# version 07 (final)
# baseline (speedup 1.0000x reference)
import os
from contextlib import ExitStack

import numpy as np
import ml_dtypes
import concourse.bass as bass
import concourse.mybir as mybir
from concourse.bass_utils import run_bass_kernel_spmd

F32 = mybir.dt.float32
BF16 = mybir.dt.bfloat16
I32 = mybir.dt.int32
U8 = mybir.dt.uint8
AF = mybir.ActivationFunctionType
ALU = mybir.AluOpType
ET = mybir.EngineType

W = 8
D = 2048
NCH = 16
T = 8192
TC = 1024
H = 16
HPC = 2
HD = 128
HID = 5632
NHC = 44
EPS = 1e-5
LN128 = float(np.log(128.0))


def f32c(v):
    return np.float32(v)

ENGS = ("sync", "gpsimd", "vector", "scalar", "tensor")


class Sched:

    def __init__(self, nc, semnames):
        self.nc = nc
        self.semnames = list(semnames)
        self.cnt = {s: 0 for s in semnames}
        self.ops = {e: [] for e in ENGS}
        self.lastwait = {e: {s: 0 for s in semnames} for e in ENGS}

    def on(self, eng, fn, wait=(), inc=None, inc_amt=1):
        w = []
        for s, v in wait:
            if v is None or v <= 0:
                continue
            if v > self.lastwait[eng][s]:
                self.lastwait[eng][s] = v
                w.append((s, v))
        tok = None
        if inc is not None:
            self.cnt[inc] += inc_amt
            tok = self.cnt[inc]
        self.ops[eng].append((fn, w, inc, inc_amt))
        return tok

    def dma(self, eng, out, in_, sem, wait=()):
        return self.on(eng, lambda E, o=out, i=in_: E.dma_start(out=o, in_=i),
                       wait=wait, inc=sem, inc_amt=16)

    def cur(self, sem):
        return self.cnt[sem]

    def replay(self, block, sems):
        for e in ENGS:
            ops = self.ops[e]

            def runner(engine, ops=ops):
                for fn, w, inc, amt in ops:
                    for s, v in w:
                        engine.wait_ge(sems[s], v)
                    inst = fn(engine)
                    if inc is not None:
                        inst.then_inc(sems[inc], amt)

            getattr(block, e)(runner)


def build_program(upto=None):
    if upto is None:
        upto = int(os.environ.get('KUPTO', '99'))
    nc = bass.Bass("TRN2", target_bir_lowering=False,
                   detect_race_conditions=False)

    d_xT = nc.dram_tensor("xT", [D, TC], F32, kind="ExternalInput")
    d_wqT = nc.dram_tensor("wqT", [D, HPC * HD], F32, kind="ExternalInput")
    d_wkT = nc.dram_tensor("wkT", [D, HPC * HD], F32, kind="ExternalInput")
    d_wvT = nc.dram_tensor("wvT", [D, HPC * HD], F32, kind="ExternalInput")
    d_woT = nc.dram_tensor("woT", [D, D], F32, kind="ExternalInput")
    d_wmodc = nc.dram_tensor("wmodc", [256, 6 * D], F32, kind="ExternalInput")
    d_cslice = nc.dram_tensor("cslice", [128, 2], F32, kind="ExternalInput")
    d_gq = nc.dram_tensor("gq", [128, 1], F32, kind="ExternalInput")
    d_gk = nc.dram_tensor("gk", [128, 1], F32, kind="ExternalInput")
    d_wgT = nc.dram_tensor("wgT", [D, HID], I32, kind="ExternalInput")
    d_wuT = nc.dram_tensor("wuT", [D, HID], I32, kind="ExternalInput")
    d_wdT = nc.dram_tensor("wdT", [HID, D], I32, kind="ExternalInput")
    d_sg = nc.dram_tensor("sg", [128, NHC], F32, kind="ExternalInput")
    d_su = nc.dram_tensor("su", [128, NHC], F32, kind="ExternalInput")
    d_sd = nc.dram_tensor("sdw", [128, NCH], F32, kind="ExternalInput")
    d_out = nc.dram_tensor("outT", [D, TC], F32, kind="ExternalOutput")

    d_modin = nc.dram_tensor("modin", [128, 96], F32)
    d_modout = nc.dram_tensor("modout", [128, 96], F32)
    d_h1in = nc.dram_tensor("h1in", [D, TC], BF16)
    d_h1out = nc.dram_tensor("h1out", [W * D, TC], BF16)
    d_aoin = nc.dram_tensor("aoin", [HPC * T, HD], BF16)
    d_aoout = nc.dram_tensor("aoout", [W * HPC * T, HD], BF16)
    d_x1s = nc.dram_tensor("x1s", [D, TC], F32)

    c_ident = nc.inline_tensor(np.eye(128, dtype=ml_dtypes.bfloat16), "c_ident")
    c_mask = nc.inline_tensor(
        np.triu(np.ones((128, 128), dtype=ml_dtypes.bfloat16)), "c_mask")

    es = ExitStack()

    def sbuf(name, shape, dtype):
        return es.enter_context(nc.sbuf_tensor(name, shape, dtype))

    ident = sbuf("ident", [128, 128], BF16)
    mask01 = sbuf("mask01", [128, 128], BF16)
    ones_b = sbuf("ones_b", [128, 1], BF16)
    ones_f = sbuf("ones_f", [128, 1], F32)
    onesrow_f = sbuf("onesrow_f", [1, 128], F32)
    c128b = sbuf("c128b", [1, 128], BF16)
    quarter = sbuf("quarter", [128, 1], F32)
    negq = sbuf("negq", [128, 1], F32)
    neghalf = sbuf("neghalf", [128, 1], F32)
    zerocol = sbuf("zerocol", [128, 1], F32)
    negl128 = sbuf("negl128", [128, 1], F32)
    cvec = sbuf("cvec", [128, 2], F32)
    scv = sbuf("scv", [128, 2], F32)
    modv = sbuf("modv", [128, 96], F32)
    modx = sbuf("modx", [128, 96], F32)
    gq = sbuf("gq_s", [128, 1], F32)
    gk = sbuf("gk_s", [128, 1], F32)
    sg = sbuf("sg_s", [128, NHC], F32)
    su = sbuf("su_s", [128, NHC], F32)
    sdw = sbuf("sdw_s", [128, NCH], F32)
    wq_sb = sbuf("wq_sb", [128, NCH * 256], BF16)
    wk_sb = sbuf("wk_sb", [128, NCH * 256], BF16)
    wv_sb = sbuf("wv_sb", [128, NCH * 256], BF16)
    rowsS = sbuf("rowsS", [1, 1536], F32)
    rows_is = sbuf("rows_is", [1, 1024], F32)
    rows_mis = sbuf("rows_mis", [1, 1024], F32)
    xstage = sbuf("xstage", [128, 1024], F32)
    resstage = sbuf("resstage", [128, 1024], F32)
    ttmp = sbuf("ttmp", [128, 1024], F32)
    pbuf = sbuf("pbuf", [128, 1024], BF16)
    obuf = sbuf("obuf", [128, 1024], BF16)
    sqb = sbuf("sqb", [128, 512], BF16)
    gateb = sbuf("gateb", [128, 512], BF16)
    x1bf = sbuf("x1bf", [128, 512], BF16)
    x1sq = sbuf("x1sq", [128, 512], BF16)
    rden = sbuf("rden", [128, 1], F32)
    wgbuf = sbuf("wgbuf", [128, 2 * 2048], BF16)
    wubuf = sbuf("wubuf", [128, 2 * 2048], BF16)
    modpre = sbuf("modpre", [128, 96], F32)

    AR_BYTES = 131392
    arena = sbuf("arena", [128, AR_BYTES], U8)

    def ar(off, nbytes, dtype):
        assert off % 64 == 0 and off + nbytes <= AR_BYTES
        return arena[:, off:off + nbytes].bitcast(dtype)

    x_fm = ar(0, NCH * 1024 * 4, F32)
    h1own = ar(65536, NCH * 1024 * 2, BF16)
    wmodb = [ar(98304 + i * 12288, 12288, F32) for i in range(2)]
    qhat = [ar(h * 16384, 16384, BF16) for h in range(2)]
    khat = [ar(32768 + h * 16384, 16384, BF16) for h in range(2)]
    vtm = [ar(65536 + h * 16576, 64 * 129 * 2, BF16) for h in range(2)]
    h1st = [ar(98560 + i * 16384, 16384, BF16) for i in range(2)]
    att = [ar(g * 2048, 2048, BF16) for g in range(16)]
    afm = [ar(32768 + g * 2048, 2048, BF16) for g in range(16)]
    wobuf = [ar(65536 + i * 4096, 4096, BF16) for i in range(2)]
    h2v = ar(98560, NCH * 1024 * 2, BF16)
    guv = [ar(hc * 2048, 2048, BF16) for hc in range(NHC)]
    wdbuf = [ar(90112 + i * 11264, 11264, BF16) for i in range(2)]

    B = [es.enter_context(nc.psum_tensor(f"B{i}", [128, 512], F32))
         for i in range(7)]
    BT = es.enter_context(nc.psum_tensor("BT", [128, 1024], BF16))

    semnames = ["s0", "g0", "gm", "gwm0", "gwm1", "gh0", "gh1", "gw0", "gw1",
                "gg0", "gg1", "gu0", "gu1", "gwd0", "gwd1",
                "sao", "sat", "sx0", "sx1", "sxw", "so",
                "cc", "tn", "ve", "ac"]
    sems = {s: es.enter_context(nc.semaphore(s)) for s in semnames}
    S = Sched(nc, semnames)
    state = {}

    MM = lambda out, lhsT, rhs, st, sp: (
        lambda E, o=out, l=lhsT, r=rhs, a=st, b=sp:
            E.matmul(o, l, r, start=a, stop=b))

    t_const = []
    for dst, src in [(ident[:, :], c_ident.ap()), (mask01[:, :], c_mask.ap()),
                     (cvec[:, :], d_cslice.ap()), (gq[:, :], d_gq.ap()),
                     (gk[:, :], d_gk.ap()), (sg[:, :], d_sg.ap()),
                     (su[:, :], d_su.ap()), (sdw[:, :], d_sd.ap())]:
        t_const.append(S.dma("sync", dst, src, "s0"))
    xT3 = d_xT.ap().rearrange("(k p) t -> p k t", p=128)
    x_fm3 = x_fm.rearrange("p (k t) -> p k t", k=NCH)
    t_x = []
    for i in range(4):
        t_x.append(S.dma("sync", x_fm3[:, 4 * i:4 * i + 4, :],
                         xT3[:, 4 * i:4 * i + 4, :], "s0"))
    t_wqkv = []
    for dst, src in [(wq_sb, d_wqT), (wk_sb, d_wkT), (wv_sb, d_wvT)]:
        t_wqkv.append(S.dma(
            "gpsimd", dst[:, :].rearrange("p (k h) -> p k h", k=NCH),
            src.ap().rearrange("(k p) h -> p k h", p=128), "g0"))

    S.on("vector", lambda E: E.memset(ones_b[:, :], 1.0), inc="ve")
    S.on("vector", lambda E: E.memset(ones_f[:, :], 1.0), inc="ve")
    t_onesrow = S.on("vector", lambda E: E.memset(onesrow_f[:, :], 1.0), inc="ve")
    t_c128 = S.on("vector", lambda E: E.memset(c128b[:, :], -128.0), inc="ve")
    S.on("vector", lambda E: E.memset(quarter[:, :], 0.25))
    S.on("vector", lambda E: E.memset(negq[:, :], -0.25))
    S.on("vector", lambda E: E.memset(neghalf[:, :], -0.5))
    S.on("vector", lambda E: E.memset(zerocol[:, :], 0.0))
    t_actconst = S.on("vector", lambda E: E.memset(
        negl128[:, :], -0.5 * LN128), inc="ve")
    t_scv = S.on("scalar",
                 lambda E: E.activation(scv[:, :], cvec[:, :], AF.Silu),
                 wait=[("s0", S.cur("s0"))], inc="ac")

    wm3 = d_wmodc.ap().rearrange("(k p) j -> p k j", p=128)
    t_wm = [None, None]
    t_mmwm = [None, None]
    for piece in range(8):
        pp = piece % 2
        wmb = wmodb[pp].rearrange("p (k j) -> p k j", k=2)
        t_wm[pp] = S.dma("sync", wmb,
                         wm3[:, :, piece * 1536:(piece + 1) * 1536],
                         "gwm0" if pp == 0 else "gwm1",
                         wait=[("tn", t_mmwm[pp])])
        for g in range(12):
            gcol = piece * 12 + g
            for k2 in range(2):
                tok = S.on("tensor",
                           MM(B[0][:, gcol:gcol + 1],
                              wmb[:, k2, g * 128:(g + 1) * 128],
                              scv[:, k2:k2 + 1], k2 == 0, k2 == 1),
                           wait=[("gwm0" if pp == 0 else "gwm1", t_wm[pp]),
                                 ("ac", t_scv)],
                           inc="tn" if k2 == 1 else None)
        t_mmwm[pp] = tok
    t_modpre = S.on("vector",
                    lambda E: E.tensor_copy(modpre[:, :], B[0][:, 0:96]),
                    wait=[("tn", t_mmwm[0]), ("tn", t_mmwm[1])], inc="ve")
    t_modin = S.dma("gpsimd", d_modin.ap(), modpre[:, :], "gm",
                    wait=[("ve", t_modpre)])
    t_ccmod = S.on("gpsimd",
                   lambda E: E.collective_compute(
                       "AllReduce", ALU.add, replica_groups=[list(range(W))],
                       ins=[d_modin.ap().opt()], outs=[d_modout.ap().opt()]),
                   wait=[("gm", t_modin)], inc="cc")
    t_modv = S.dma("gpsimd", modv[:, :], d_modout.ap(), "gm",
                   wait=[("cc", t_ccmod)])

    sa_c, ca_c, ga_c = modv[:, 0:16], modv[:, 16:32], modv[:, 32:48]
    sb_c, cb_c, gb_c = modv[:, 48:64], modv[:, 64:80], modv[:, 80:96]
    oneca, onecb = modx[:, 0:16], modx[:, 16:32]
    gac, gbsd = modx[:, 32:48], modx[:, 48:64]
    mtmp, mtmp2 = modx[:, 64:80], modx[:, 80:96]
    wv_mod = [("gm", t_modv), ("s0", S.cur("s0"))]
    S.on("vector", lambda E: E.tensor_scalar_add(oneca, ca_c, 1.0), wait=wv_mod)
    S.on("vector", lambda E: E.tensor_scalar_add(onecb, cb_c, 1.0))
    t_clamp_mult = None
    for g_in, g_out, post_sdw in [(ga_c, gac, False), (gb_c, gbsd, True)]:
        S.on("scalar", lambda E, a=g_in: E.activation(mtmp, a, AF.Sign),
             wait=wv_mod + [("ve", t_clamp_mult)])
        tci = S.on("scalar", lambda E, a=g_in, o=g_out: E.activation(
            o, a, AF.Abs), inc="ac")
        tci = S.on("scalar", lambda E, o=g_out: E.activation(
            mtmp2, o, AF.Relu, bias=negq[:, 0:1]), inc="ac")
        tadd = S.on("vector", lambda E, o=g_out: E.tensor_scalar_add(
            o, mtmp2, quarter[:, 0:1]), wait=[("ac", tci)], inc="ve")
        t_clamp_mult = S.on("vector", lambda E, o=g_out: E.tensor_tensor(
            o, o, mtmp, ALU.mult), wait=[("ve", tadd)], inc="ve")
        if post_sdw:
            S.on("vector", lambda E, o=g_out: E.tensor_tensor(
                o, o, sdw[:, :], ALU.mult), wait=[("ve", t_clamp_mult)])
    t_mod_done = S.on("vector", lambda E: E.engine_nop(), inc="ve")

    RM, RV, RT = slice(0, 512), slice(512, 1024), slice(1024, 1536)
    cs_hi = rowsS[0:1, RM].bitcast(BF16)
    cs_lo = rowsS[0:1, RV].bitcast(BF16)
    cs_t = rowsS[0:1, RT]

    def emit_hilo(csrow, tb2, wait):
        c0 = tb2 * 512
        S.on("vector", lambda E: E.tensor_copy(
            cs_hi[0:1, c0:c0 + 512], csrow[0:1, c0:c0 + 512]), wait=wait)
        S.on("vector", lambda E: E.tensor_tensor(
            cs_t[0:1, 0:512], csrow[0:1, c0:c0 + 512], cs_hi[0:1, c0:c0 + 512],
            ALU.subtract))
        return S.on("vector", lambda E: E.tensor_copy(
            cs_lo[0:1, c0:c0 + 512], cs_t[0:1, 0:512]), inc="ve")
    csum_h2 = rows_is
    csum_gu = rows_mis

    def ln_rows(tb, t_sumtok, t_sqtok, sb_sum=None, sb_sq=None):
        sb_sum = B[4] if sb_sum is None else sb_sum
        sb_sq = B[5] if sb_sq is None else sb_sq
        c0, c1 = tb * 512, tb * 512 + 512
        t1 = S.on("vector", lambda E, b=sb_sum: E.tensor_scalar(
            rowsS[0:1, RM], b[0:1, 0:512], 1.0 / D, None, ALU.mult),
            wait=[("tn", t_sumtok), ("ve", state.get("rowsSu"))], inc="ve")
        t2 = S.on("vector", lambda E, b=sb_sq: E.tensor_scalar(
            rowsS[0:1, RV], b[0:1, 0:512], 1.0 / D, None, ALU.mult),
            wait=[("tn", t_sqtok)], inc="ve")
        S.on("vector", lambda E: E.tensor_tensor(
            rowsS[0:1, RT], rowsS[0:1, RM], rowsS[0:1, RM], ALU.mult))
        S.on("vector", lambda E: E.tensor_tensor(
            rowsS[0:1, RV], rowsS[0:1, RV], rowsS[0:1, RT], ALU.subtract))
        t3 = S.on("vector", lambda E: E.tensor_scalar_add(
            rowsS[0:1, RV], rowsS[0:1, RV], EPS), inc="ve")
        t4 = S.on("scalar", lambda E: E.activation(
            rowsS[0:1, RT], rowsS[0:1, RV], AF.Ln),
            wait=[("ve", t3)], inc="ac")
        t5 = S.on("scalar", lambda E: E.activation(
            rows_is[0:1, c0:c1], rowsS[0:1, RT], AF.Exp,
            scale=neghalf[0:1, 0:1]),
            wait=[("ac", t4), ("ve", t_actconst)], inc="ac")
        t6 = S.on("vector", lambda E: E.tensor_tensor(
            rows_mis[0:1, c0:c1], rowsS[0:1, RM], rows_is[0:1, c0:c1],
            ALU.mult),
            wait=[("ac", t5)], inc="ve")
        state["rowsSu"] = t6
        return t6

    def ln_bcast(tb, t_rows, extra_wait=()):
        c0, c1 = tb * 512, tb * 512 + 512
        w = [("ve", t_rows), ("ve", t_onesrow)] + list(extra_wait)
        tis = S.on("tensor", MM(B[2][:, 0:512], onesrow_f[:, :],
                                rows_is[0:1, c0:c1], True, True),
                   wait=w, inc="tn")
        tmis = S.on("tensor", MM(B[3][:, 0:512], onesrow_f[:, :],
                                 rows_mis[0:1, c0:c1], True, True), inc="tn")
        return tis, tmis

    if upto >= 1:
        h1v = h1own
        t_apply = []
        for tb in range(2):
            t_mm_sum = t_mm_sq = None
            prev_ve = [None, None]
            for k in range(NCH):
                sl = slice(k * 1024 + tb * 512, k * 1024 + tb * 512 + 512)
                ss = k % 2
                tsq = S.on("vector", lambda E, s=sl, z=ss: E.tensor_tensor(
                    ttmp[:, z * 512:z * 512 + 512], x_fm[:, s], x_fm[:, s], ALU.mult),
                    wait=[("s0", S.cur("s0")), ("tn", prev_ve[ss])], inc="ve")
                t_mm_sum = S.on("tensor", MM(B[4][0:1, 0:512], ones_f[:, :],
                                             x_fm[:, sl], k == 0, k == NCH - 1),
                                wait=[("ve", tsq)],
                                inc="tn" if k == NCH - 1 else None)
                t_mm_sq = S.on("tensor", MM(
                    B[5][0:1, 0:512], ones_f[:, :],
                    ttmp[:, ss * 512:ss * 512 + 512], k == 0, k == NCH - 1),
                    inc="tn")
                prev_ve[ss] = t_mm_sq
            t_rows = ln_rows(tb, t_mm_sum, t_mm_sq)
            tis, tmis = ln_bcast(tb, t_rows,
                                 extra_wait=[("ve", state.get("ln1ap"))])
            prev_ac = [None, None]
            for k in range(NCH):
                sl = slice(k * 1024 + tb * 512, k * 1024 + tb * 512 + 512)
                ss = k % 2
                tv = S.on("vector", lambda E, s=sl, z=ss: E.tensor_tensor(
                    ttmp[:, z * 512:z * 512 + 512], x_fm[:, s], B[2][:, 0:512],
                    ALU.mult),
                    wait=[("tn", tmis), ("ac", prev_ac[ss])], inc="ve")
                tv = S.on("vector", lambda E, z=ss: E.tensor_tensor(
                    ttmp[:, z * 512:z * 512 + 512],
                    ttmp[:, z * 512:z * 512 + 512], B[3][:, 0:512], ALU.subtract),
                    inc="ve")
                ta = S.on("scalar", lambda E, s=sl, z=ss, kk=k: E.activation(
                    h1v[:, s], ttmp[:, z * 512:z * 512 + 512], AF.Identity,
                    bias=sa_c[:, kk:kk + 1], scale=oneca[:, kk:kk + 1]),
                    wait=[("ve", tv), ("ve", t_mod_done)], inc="ac")
                prev_ac[ss] = ta
                t_apply.append(ta)
                state["ln1ap"] = tv

    if upto >= 2:
        h1in3 = d_h1in.ap().rearrange("(k p) t -> p k t", p=128)
        h1own3 = h1own.rearrange("p (k t) -> p k t", k=NCH)
        t_h1w = []
        for i in range(2):
            t_h1w.append(S.dma("sync", h1in3[:, 8 * i:8 * i + 8, :],
                               h1own3[:, 8 * i:8 * i + 8, :], "s0",
                               wait=[("ac", t) for t in t_apply]))
        t_cch1 = S.on("gpsimd", lambda E: E.collective_compute(
            "AllGather", ALU.bypass, replica_groups=[list(range(W))],
            ins=[d_h1in.ap().opt()], outs=[d_h1out.ap().opt()]),
            wait=[("s0", t_h1w[-1])], inc="cc")
        t_vone = []
        for h in range(2):
            t_vone.append(S.on("vector", lambda E, h=h: E.memset(vtm[h], 1.0),
                               wait=[("s0", t_h1w[-1])], inc="ve"))

    if upto >= 3:
        h1o4 = d_h1out.ap().rearrange("(r k p) t -> r p k t", r=W, p=128)
        t_h1st = [None, None]
        last_use_h1st = [None, None]
        t_qkv_done = []
        for r in range(W):
            for tb2 in range(2):
                step = r * 2 + tb2
                pp = step % 2
                h1s = h1st[pp].rearrange("p (k t) -> p k t", k=NCH)
                src = h1o4[r][:, :, tb2 * 512:tb2 * 512 + 512]
                hsem = "gh0" if pp == 0 else "gh1"
                t_h1st[pp] = S.dma("gpsimd", h1s[:, 0:8, :], src[:, 0:8, :], hsem,
                                   wait=[("cc", t_cch1),
                                         ("tn", last_use_h1st[pp])])
                t_h1st[pp] = S.dma("gpsimd", h1s[:, 8:16, :], src[:, 8:16, :], hsem)
                col = r * 1024 + tb2 * 512
                csl = slice(col, col + 512)
                for gidx, (hh, which) in enumerate(
                        [(h, w) for h in range(HPC) for w in ("q", "k")]):
                        wsb = wq_sb if which == "q" else wk_sb
                        dest = qhat[hh] if which == "q" else khat[hh]
                        gvec = gq if which == "q" else gk
                        ebias = 0.0 if which == "q" else -0.5 * LN128
                        bank = B[gidx % 2]
                        for k in range(NCH):
                            tmm = S.on("tensor", MM(
                                bank[:, 0:512],
                                wsb[:, k * 256 + hh * 128:k * 256 + hh * 128 + 128],
                                h1s[:, k, :], k == 0, k == NCH - 1),
                                wait=[("gh0" if pp == 0 else "gh1", t_h1st[pp]),
                                      ("g0", 48),
                                      ("ac", state.get(("evac", gidx % 2)))],
                                inc="tn" if k == NCH - 1 else None)
                        tev = S.on("scalar", lambda E, d=dest, c=csl, b=bank:
                                   E.activation(d[:, c], b[:, 0:512], AF.Identity),
                                   wait=[("tn", tmm)], inc="ac")
                        state[("evac", gidx % 2)] = tev
                        tsq = S.on("vector", lambda E, d=dest, c=csl: E.tensor_tensor(
                            sqb[:, 0:512], d[:, c], d[:, c], ALU.mult),
                            wait=[("ac", tev), ("tn", state.get("sqb"))], inc="ve")
                        trs = S.on("tensor", MM(B[4][0:1, 0:512], ones_b[:, :],
                                                sqb[:, 0:512], True, True),
                                   wait=[("ve", tsq), ("ve", state.get("rowp"))],
                                   inc="tn")
                        state["sqb"] = trs
                        S.on("vector", lambda E: E.tensor_scalar_mul(
                            rowsS[0:1, RM], B[4][0:1, 0:512], 1.0 / HD),
                            wait=[("tn", trs)])
                        trp = S.on("vector", lambda E: E.tensor_scalar_add(
                            rowsS[0:1, RM], rowsS[0:1, RM], EPS), inc="ve")
                        state["rowp"] = trp
                        tln = S.on("scalar", lambda E: E.activation(
                            rowsS[0:1, RV], rowsS[0:1, RM], AF.Ln),
                            wait=[("ve", trp)], inc="ac")
                        bcol = zerocol if which == "q" else negl128
                        trs2 = S.on("scalar", lambda E, bc=bcol: E.activation(
                            rows_is[0:1, 0:512], rowsS[0:1, RV], AF.Exp,
                            scale=neghalf[0:1, 0:1], bias=bc[0:1, 0:1]), inc="ac")
                        tbc = S.on("tensor", MM(B[5][:, 0:512], onesrow_f[:, :],
                                                rows_is[0:1, 0:512], True, True),
                                   wait=[("ac", trs2), ("ve", t_onesrow),
                                         ("ve", state.get("B5u"))], inc="tn")
                        tap = S.on("vector", lambda E, d=dest, c=csl, g=gvec:
                                   E.scalar_tensor_tensor(
                                       d[:, c], d[:, c], g[:, 0:1], B[5][:, 0:512],
                                       ALU.mult, ALU.mult),
                                   wait=[("tn", tbc)], inc="ve")
                        state["B5u"] = tap
                        t_qkv_done.append(tap)
                for j in range(4):
                    gt = r * 8 + tb2 * 4 + j
                    for k in range(NCH):
                        tvm = S.on("tensor", MM(
                            B[2][:, 0:256],
                            h1s[:, k, j * 128:j * 128 + 128],
                            wv_sb[:, k * 256:(k + 1) * 256], k == 0, k == NCH - 1),
                            wait=[("gh0" if pp == 0 else "gh1", t_h1st[pp]),
                                  ("g0", 48),
                                  ("ve", state.get("B2u"))],
                            inc="tn" if k == NCH - 1 else None)
                    for hh in range(2):
                        tvev = S.on("vector", lambda E, h=hh, g=gt: E.tensor_copy(
                            vtm[h][:, g * 129:g * 129 + 128],
                            B[2][:, h * 128:h * 128 + 128]),
                            wait=[("tn", tvm), ("ve", t_vone[hh])], inc="ve")
                        t_qkv_done.append(tvev)
                    state["B2u"] = tvev
                last_use_h1st[pp] = tvm

    if upto >= 4:
        aoin5 = d_aoin.ap().rearrange("(hh qb j p) h -> hh qb p j h", hh=2, qb=16, j=4)
        w_qkv = [("ve", t) for t in t_qkv_done]
        t_ao = []
        state["avband"] = [None] * 4
        state["pbufu"] = [None, None]
        state["sbank"] = [state.get(("evac", 0)), state.get(("evac", 1))]
        t_obuf_dma = [None, None]
        for hh in range(HPC):
            for qb in range(16):
                ntk = (qb + 1) * 4
                av_stop = [None] * 4
                for tk in range(ntk):
                    pp = tk % 2
                    wmm = [("ac", state["sbank"][pp])]
                    if tk == 0 and qb == 0:
                        wmm += w_qkv
                    tsm = S.on("tensor", MM(B[pp][:, 0:512],
                                      khat[hh][:, tk * 128:tk * 128 + 128],
                                      qhat[hh][:, qb * 512:qb * 512 + 512],
                                      True, True), wait=wmm, inc="tn")
                    texp = S.on("scalar", lambda E, z=pp: E.activation(
                        pbuf[:, z * 512:z * 512 + 512], B[z][:, 0:512], AF.Exp),
                        wait=[("tn", tsm), ("tn", state["pbufu"][pp])], inc="ac")
                    state["sbank"][pp] = texp
                    j = tk - qb * 4
                    tmask = None
                    if j >= 0:
                        tmask = S.on("vector", lambda E, z=pp, jj=j: E.tensor_tensor(
                            pbuf[:, z * 512 + jj * 128:z * 512 + jj * 128 + 128],
                            pbuf[:, z * 512 + jj * 128:z * 512 + jj * 128 + 128],
                            mask01[:, :], ALU.mult),
                            wait=[("ac", texp)], inc="ve")
                    j2lo = max(j, 0)
                    for j2 in range(j2lo, 4):
                        wav = [("ve", tmask)] if (tmask is not None and j2 == j) \
                            else [("ac", texp)]
                        if tk == 0:
                            wav.append(("ac", state["avband"][j2]))
                        stop = (tk == qb * 4 + j2)
                        last = (j2 == 3)
                        tok = S.on("tensor", MM(
                            B[3 + j2][:, 0:129],
                            pbuf[:, pp * 512 + j2 * 128:pp * 512 + j2 * 128 + 128],
                            vtm[hh][:, tk * 129:tk * 129 + 129],
                            tk == 0, stop),
                            wait=wav, inc="tn" if (stop or last) else None)
                        if stop:
                            av_stop[j2] = tok
                    state["pbufu"][pp] = tok
                sl = qb % 2
                ev = []
                for j2 in range(4):
                    trc = S.on("vector", lambda E, j2=j2: E.reciprocal(
                        rden[:, :], B[3 + j2][:, 128:129]),
                        wait=[("tn", av_stop[j2]), ("ac", state.get("rdenu"))],
                        inc="ve")
                    tev = S.on("scalar", lambda E, j2=j2, s=sl: E.activation(
                        obuf[:, s * 512 + j2 * 128:s * 512 + j2 * 128 + 128],
                        B[3 + j2][:, 0:128], AF.Identity, scale=rden[:, 0:1]),
                        wait=[("ve", trc), ("sao", S.cur("sao"))], inc="ac")
                    state["rdenu"] = tev
                    state["avband"][j2] = tev
                    ev.append(tev)
                t_obuf_dma[sl] = S.dma(
                    "sync", aoin5[hh, qb],
                    obuf[:, sl * 512:sl * 512 + 512].rearrange(
                        "p (j h) -> p j h", j=4), "sao",
                    wait=[("ac", t) for t in ev])
                t_ao.append(t_obuf_dma[sl])

    if upto >= 5:
        KP4 = int(os.environ.get('KP4', '9'))
        t_ccao = S.on("gpsimd", lambda E: E.collective_compute(
            "AllGather", ALU.bypass, replica_groups=[list(range(W))],
            ins=[d_aoin.ap().opt()], outs=[d_aoout.ap().opt()]),
            wait=[("sao", t_ao[-1])], inc="cc")

        def att_load_fn(g):
            def fn(E):
                if "pid_sp" not in state:
                    state["pid_sp"] = nc.partition_id([ET.SP])
                pid = state["pid_sp"]
                base = d_aoout.ap().rearrange(
                    "(g tb tt p) h -> g tb p tt h", g=16, tb=8, tt=8)[g, 0]
                ap = bass.AP(base.tensor, base.offset + pid * (TC * HD), base.ap)
                return E.dma_start(out=att[g].rearrange("p (tt h) -> p tt h", tt=8),
                                   in_=ap)
            return fn

        t_att = []
        for g in range(16 if KP4 >= 2 else 0):
            t_att.append(S.on("sync", att_load_fn(g), wait=[("cc", t_ccao)],
                              inc="sat", inc_amt=16))
        t_afm = []
        tslot = [BT[:, 0:128], B[6][:, 0:256].bitcast(BF16)[:, 0:128]]
        tp_slot = [None] * 2
        for g in range(16 if KP4 >= 3 else 0):
            for tt in range(8):
                s = (g * 8 + tt) % 2
                ttr = S.on("tensor", lambda E, g=g, tt=tt, s=s: E.transpose(
                    tslot[s], att[g][:, tt * 128:tt * 128 + 128], ident[:, :]),
                    wait=[("sat", t_att[-1] if g == 0 and tt == 0 else t_att[g]),
                          ("ve", tp_slot[s])], inc="tn")
                tp_slot[s] = S.on("vector", lambda E, g=g, tt=tt, s=s: E.tensor_copy(
                    afm[g][:, tt * 128:tt * 128 + 128], tslot[s]),
                    wait=[("tn", ttr)], inc="ve")
                t_afm.append(tp_slot[s])
        w_afm = [("ve", t) for t in t_afm]

        wo4 = d_woT.ap().rearrange("(g p) d -> p g d", p=128)
        x1s3 = d_x1s.ap().rearrange("(k p) t -> p k t", p=128)
        t_wo = [None, None]
        t_wo_used = [None, None]
        t_x1w = [None, None]
        t_xs = [None, None]
        t_res = [None, None]
        t_st_sum = {0: None, 1: None}
        t_st_sq = {0: None, 1: None}
        stat_banks = {0: (B[4], B[5]), 1: (B[2], B[3])}
        for dc in range(NCH if KP4 >= 4 else 0):
            pp = dc % 2
            t_wo[pp] = S.dma("gpsimd", wobuf[pp].rearrange("p (g d) -> p g d", g=16),
                             wo4[:, :, dc * 128:dc * 128 + 128],
                             "gw0" if pp == 0 else "gw1",
                             wait=[("tn", t_wo_used[pp])])
            for tb2 in range(2):
                it = dc * 2 + tb2
                bank = B[it % 2]
                for g in range(16):
                    tmm = S.on("tensor", MM(
                        bank[:, 0:512], wobuf[pp][:, g * 128:g * 128 + 128],
                        afm[g][:, tb2 * 512:tb2 * 512 + 512], g == 0, g == 15),
                        wait=[("gw0" if pp == 0 else "gw1", t_wo[pp]),
                              ("ve", state.get(("wores", it % 2)))] +
                             (w_afm if it == 0 else []),
                        inc="tn" if g == 15 else None)
                if tb2 == 1:
                    t_wo_used[pp] = tmm
                sx = it % 2
                t_xs[sx] = S.dma("sync", xstage[:, sx * 512:sx * 512 + 512],
                                 xT3[:, dc, tb2 * 512:tb2 * 512 + 512],
                                 "sx0" if sx == 0 else "sx1",
                                 wait=[("ve", state.get(("xst", sx)))])
                t_res[sx] = S.on("vector", lambda E, b=bank, s=sx, k=dc:
                                 E.scalar_tensor_tensor(
                                     resstage[:, s * 512:s * 512 + 512],
                                     b[:, 0:512], gac[:, k:k + 1],
                                     xstage[:, s * 512:s * 512 + 512],
                                     ALU.mult, ALU.add),
                                 wait=[("tn", tmm),
                                       ("sx0" if sx == 0 else "sx1", t_xs[sx]),
                                       ("ve", t_mod_done),
                                       ("sxw", S.cur("sxw"))], inc="ve")
                state[("xst", sx)] = t_res[sx]
                state[("wores", it % 2)] = t_res[sx]
                tbf = S.on("vector", lambda E, s=sx: E.tensor_copy(
                    x1bf[:, :], resstage[:, s * 512:s * 512 + 512]),
                    wait=[("tn", state.get("x1bfu"))], inc="ve")
                tsq2 = S.on("vector", lambda E: E.tensor_tensor(
                    x1sq[:, :], x1bf[:, :], x1bf[:, :], ALU.mult), inc="ve")
                sb_sum, sb_sq = stat_banks[tb2]
                t_st_sum[tb2] = S.on("tensor", MM(
                    sb_sum[0:1, 0:512], ones_b[:, :], x1bf[:, :],
                    dc == 0, dc == NCH - 1),
                    wait=[("ve", tsq2)], inc="tn" if dc == NCH - 1 else None)
                t_st_sq[tb2] = S.on("tensor", MM(
                    sb_sq[0:1, 0:512], ones_b[:, :], x1sq[:, :],
                    dc == 0, dc == NCH - 1),
                    inc="tn" if dc == NCH - 1 else None)
                state["x1bfu"] = t_st_sq[tb2]
                t_x1w[sx] = S.dma("sync", x1s3[:, dc, tb2 * 512:tb2 * 512 + 512],
                                  resstage[:, sx * 512:sx * 512 + 512], "sxw",
                                  wait=[("ve", t_res[sx])])

    if os.environ.get("KDBG") == "1" and upto >= 5:
        d_dbg_afm = nc.dram_tensor("dbg_afm", [128, 16 * 1024], BF16,
                                   kind="ExternalOutput")
        S.dma("gpsimd", d_dbg_afm.ap(),
              arena[:, 32768:65536].bitcast(BF16), "gm",
              wait=[(s_, S.cnt[s_]) for s_ in ("ve", "tn", "ac")])
    if upto >= 6:
        h2_3 = h2v
        t_h2 = []
        t_rows5 = []
        for tb in range(2):
            sb_sum, sb_sq = stat_banks[tb]
            t_rows5.append(ln_rows(tb, t_st_sum[tb], t_st_sq[tb],
                                   sb_sum=sb_sum, sb_sq=sb_sq))
        last_apply = None
        for tb in range(2):
            c0, c1 = tb * 512, tb * 512 + 512
            tis, tmis = ln_bcast(tb, t_rows5[tb],
                                 extra_wait=[("ve", last_apply),
                                             ("ve", t_rows5[-1])])
            prev_ac = [None, None]
            for k in range(NCH):
                sx = k % 2
                t_xs[sx] = S.dma("sync", xstage[:, sx * 512:sx * 512 + 512],
                                 x1s3[:, k, tb * 512:tb * 512 + 512],
                                 "sx0" if sx == 0 else "sx1",
                                 wait=[("ve", state.get(("xst", sx))),
                                       ("sxw", S.cur("sxw"))])
                tv = S.on("vector", lambda E, s=sx: E.tensor_tensor(
                    ttmp[:, s * 512:s * 512 + 512],
                    xstage[:, s * 512:s * 512 + 512], B[2][:, 0:512], ALU.mult),
                    wait=[("sx0" if sx == 0 else "sx1", t_xs[sx]), ("tn", tmis),
                          ("ac", prev_ac[sx])], inc="ve")
                state[("xst", sx)] = tv
                tv = S.on("vector", lambda E, s=sx: E.tensor_tensor(
                    ttmp[:, s * 512:s * 512 + 512],
                    ttmp[:, s * 512:s * 512 + 512], B[3][:, 0:512], ALU.subtract),
                    inc="ve")
                ta = S.on("scalar", lambda E, s=sx, kk=k, c0=c0: E.activation(
                    h2_3[:, kk * 1024 + c0:kk * 1024 + c0 + 512],
                    ttmp[:, s * 512:s * 512 + 512], AF.Identity,
                    bias=sb_c[:, kk:kk + 1], scale=onecb[:, kk:kk + 1]),
                    wait=[("ve", tv)], inc="ac")
                prev_ac[sx] = ta
                t_h2.append(ta)
                last_apply = tv
        w_h2 = [("ac", t) for t in t_h2]

    if upto >= 7:
        wg4 = d_wgT.ap().rearrange("(k p) h -> p k h", p=128)
        wu4 = d_wuT.ap().rearrange("(k p) h -> p k h", p=128)
        t_gu = []
        t_wg = [None, None]
        t_wu = [None, None]
        t_wg_used = [None, None]
        for hc in range(NHC):
            pp = hc % 2
            wgs = wgbuf[:, pp * 2048:pp * 2048 + 2048]
            wus = wubuf[:, pp * 2048:pp * 2048 + 2048]
            t_wg[pp] = S.dma("gpsimd", wgs.rearrange("p (k h) -> p k h", k=16),
                             wg4[:, :, hc * 128:hc * 128 + 128],
                             "gg0" if pp == 0 else "gg1",
                             wait=[("tn", t_wg_used[pp])])
            t_wu[pp] = S.dma("gpsimd", wus.rearrange("p (k h) -> p k h", k=16),
                             wu4[:, :, hc * 128:hc * 128 + 128],
                             "gu0" if pp == 0 else "gu1",
                             wait=[("tn", state.get(("wuu", pp)))])
            if hc == 0:
                for tb2 in range(2):
                    cb = B[2 + tb2]
                    for k in range(NCH):
                        tcs = S.on("tensor", MM(
                            cb[0:1, 0:512], ones_b[:, :],
                            h2_3[:, k * 1024 + tb2 * 512:k * 1024 + tb2 * 512 + 512],
                            k == 0, k == NCH - 1),
                            wait=(w_h2 if tb2 == 0 and k == 0 else []),
                            inc="tn" if k == NCH - 1 else None)
                    S.on("vector", lambda E, b2=tb2, cb=cb: E.tensor_copy(
                        csum_h2[0:1, b2 * 512:b2 * 512 + 512], cb[0:1, 0:512]),
                        wait=[("tn", tcs)])
                    tev = emit_hilo(csum_h2, tb2, [])
                state["csum_done"] = tev
            for tb2 in range(2):
                for k in range(NCH):
                    tg = S.on("tensor", MM(
                        B[0][:, 0:512], wgs[:, k * 128:k * 128 + 128],
                        h2_3[:, k * 1024 + tb2 * 512:k * 1024 + tb2 * 512 + 512],
                        k == 0, False),
                        wait=[("gg0" if pp == 0 else "gg1", t_wg[pp]),
                              ("ac", state.get("siluB0")),
                              ("ve", state.get("csum_done")), ("ve", t_c128)],
                        inc=None)
                S.on("tensor", MM(
                    B[0][:, 0:512], c128b[:, :],
                    cs_hi[0:1, tb2 * 512:tb2 * 512 + 512], False, False))
                tg = S.on("tensor", MM(
                    B[0][:, 0:512], c128b[:, :],
                    cs_lo[0:1, tb2 * 512:tb2 * 512 + 512], False, True),
                    inc="tn")
                for k in range(NCH):
                    tu = S.on("tensor", MM(
                        B[1][:, 0:512], wus[:, k * 128:k * 128 + 128],
                        h2_3[:, k * 1024 + tb2 * 512:k * 1024 + tb2 * 512 + 512],
                        k == 0, False),
                        wait=[("gu0" if pp == 0 else "gu1", t_wu[pp]),
                              ("ve", state.get("guB1"))],
                        inc=None)
                S.on("tensor", MM(
                    B[1][:, 0:512], c128b[:, :],
                    cs_hi[0:1, tb2 * 512:tb2 * 512 + 512], False, False))
                tu = S.on("tensor", MM(
                    B[1][:, 0:512], c128b[:, :],
                    cs_lo[0:1, tb2 * 512:tb2 * 512 + 512], False, True),
                    inc="tn")
                tsl = S.on("scalar", lambda E, h=hc: E.activation(
                    gateb[:, :], B[0][:, 0:512], AF.Silu, scale=sg[:, h:h + 1]),
                    wait=[("tn", tg), ("ve", state.get("gatebu"))], inc="ac")
                state["siluB0"] = tsl
                tgu = S.on("vector", lambda E, h=hc, b2=tb2: E.scalar_tensor_tensor(
                    guv[h][:, b2 * 512:b2 * 512 + 512], B[1][:, 0:512],
                    su[:, h:h + 1], gateb[:, :], ALU.mult, ALU.mult),
                    wait=[("tn", tu), ("ac", tsl)], inc="ve")
                state["guB1"] = tgu
                state["gatebu"] = tgu
                t_gu.append(tgu)
            t_wg_used[pp] = tg
            state[("wuu", pp)] = tu
        w_gu = [("ve", t) for t in t_gu]

    if upto >= 8:
        wd4 = d_wdT.ap().rearrange("(hc p) d -> p hc d", p=128)
        t_wd = [None, None]
        t_wd_used = [None, None]
        t_os = [None, None]
        t_ow = [None, None]
        n_out = 0
        for dc in range(NCH):
            pp = dc % 2
            wds = wdbuf[pp]
            wds3 = wds.rearrange("p (hc d) -> p hc d", hc=NHC)
            t_wd[pp] = None
            for q in range(4):
                t_wd[pp] = S.dma(
                    "gpsimd", wds3[:, q * 11:q * 11 + 11, :],
                    wd4[:, q * 11:q * 11 + 11, dc * 128:dc * 128 + 128],
                    "gwd0" if pp == 0 else "gwd1",
                    wait=[("tn", t_wd_used[pp])] + (w_gu if dc == 0 else []))
            if dc == 0:
                for tb2 in range(2):
                    cb = B[4 + tb2]
                    for hc in range(NHC):
                        tcs = S.on("tensor", MM(
                            cb[0:1, 0:512], ones_b[:, :],
                            guv[hc][:, tb2 * 512:tb2 * 512 + 512],
                            hc == 0, hc == NHC - 1),
                            wait=(w_gu if tb2 == 0 and hc == 0 else []),
                            inc="tn" if hc == NHC - 1 else None)
                    S.on("vector", lambda E, b2=tb2, cb=cb: E.tensor_copy(
                        csum_gu[0:1, b2 * 512:b2 * 512 + 512], cb[0:1, 0:512]),
                        wait=[("tn", tcs)])
                    tev = emit_hilo(csum_gu, tb2, [])
                state["csum2_done"] = tev
            stops = {}
            for hc in range(NHC):
                for tb2 in range(2):
                    bank = B[tb2]
                    tok = S.on("tensor", MM(
                        bank[:, 0:512],
                        wds3[:, hc, 0:128],
                        guv[hc][:, tb2 * 512:tb2 * 512 + 512],
                        hc == 0, False),
                        wait=[("gwd0" if pp == 0 else "gwd1", t_wd[pp]),
                              ("ve", state.get(("dres", tb2))),
                              ("ve", state.get("csum2_done"))],
                        inc=None)
            for tb2 in range(2):
                S.on("tensor", MM(
                    B[tb2][:, 0:512], c128b[:, :],
                    cs_hi[0:1, tb2 * 512:tb2 * 512 + 512], False, False))
                tok = S.on("tensor", MM(
                    B[tb2][:, 0:512], c128b[:, :],
                    cs_lo[0:1, tb2 * 512:tb2 * 512 + 512], False, True),
                    inc="tn")
                stops[tb2] = tok
            t_wd_used[pp] = tok
            for tb2 in range(2):
                bank = B[tb2]
                sx = tb2
                t_os[sx] = S.dma("sync", xstage[:, sx * 512:sx * 512 + 512],
                                 x1s3[:, dc, tb2 * 512:tb2 * 512 + 512],
                                 "sx0" if sx == 0 else "sx1",
                                 wait=[("ve", state.get(("xst", sx))),
                                       ("sxw", S.cur("sxw"))])
                t_res[sx] = S.on(
                    "vector", lambda E, b=bank, s=sx, k=dc:
                    E.scalar_tensor_tensor(
                        resstage[:, s * 512:s * 512 + 512], b[:, 0:512],
                        gbsd[:, k:k + 1], xstage[:, s * 512:s * 512 + 512],
                        ALU.mult, ALU.add),
                    wait=[("tn", stops[tb2]),
                          ("sx0" if sx == 0 else "sx1", t_os[sx]),
                          ("so", S.cur("so"))], inc="ve")
                state[("xst", sx)] = t_res[sx]
                state[("dres", tb2)] = t_res[sx]
                t_ow[sx] = S.on(
                    "sync", lambda E, s=sx, k=dc, b2=tb2: E.dma_start(
                        out=d_out.ap().rearrange("(k p) t -> p k t", p=128)
                        [:, k, b2 * 512:b2 * 512 + 512],
                        in_=resstage[:, s * 512:s * 512 + 512]),
                    wait=[("ve", t_res[sx])], inc="so", inc_amt=16)
                n_out += 1

    if os.environ.get("KDBG") == "1":
        d_dbg_aoo = nc.dram_tensor("dbg_aoo", [2 * T, HD], BF16, kind="ExternalOutput")
        if upto >= 5:
            S.dma("gpsimd", d_dbg_aoo.ap(), d_aoout.ap()[0:2 * T, :], "gm",
                  wait=[("cc", S.cnt["cc"])])
        d_dbg_h1 = nc.dram_tensor("dbg_h1", [D, TC], BF16, kind="ExternalOutput")
        d_dbg_ao = nc.dram_tensor("dbg_ao", [HPC * T, HD], BF16, kind="ExternalOutput")
        d_dbg_x1 = nc.dram_tensor("dbg_x1", [D, TC], F32, kind="ExternalOutput")
        d_dbg_h2 = nc.dram_tensor("dbg_h2", [128, NCH * 1024], BF16, kind="ExternalOutput")
        d_dbg_mod = nc.dram_tensor("dbg_mod", [128, 96], F32, kind="ExternalOutput")
        d_dbg_modx = nc.dram_tensor("dbg_modx", [128, 96], F32, kind="ExternalOutput")
        wd_all = [(s, S.cnt[s]) for s in semnames if S.cnt[s]]
        if upto >= 2:
            S.dma("gpsimd", d_dbg_h1.ap(), d_h1in.ap(), "gm", wait=wd_all)
        if upto >= 4:
            S.dma("gpsimd", d_dbg_ao.ap(), d_aoin.ap(), "gm", wait=wd_all)
        if upto >= 5:
            S.dma("gpsimd", d_dbg_x1.ap(), d_x1s.ap(), "gm", wait=wd_all)
        if upto >= 6:
            S.dma("gpsimd", d_dbg_h2.ap(), h2v, "gm", wait=wd_all)
        S.dma("gpsimd", d_dbg_mod.ap(), modv[:, :], "gm", wait=wd_all)
        S.dma("gpsimd", d_dbg_modx.ap(), modx[:, :], "gm", wait=wd_all)
        if upto >= 7:
            d_dbg_gu = nc.dram_tensor("dbg_gu", [128, NHC * 1024], BF16,
                                      kind="ExternalOutput")
            S.dma("gpsimd", d_dbg_gu.ap(), arena[:, 0:90112].bitcast(BF16),
                  "gm", wait=wd_all)
            d_dbg_cs = nc.dram_tensor("dbg_cs", [1, 1024], F32,
                                      kind="ExternalOutput")
            S.dma("gpsimd", d_dbg_cs.ap(), csum_gu[0:1, :], "gm", wait=wd_all)
        S.on("gpsimd", lambda E: E.engine_nop(), wait=[("gm", S.cnt["gm"])])

    for semn in semnames:
        if S.cnt[semn]:
            S.on('gpsimd', lambda E: E.engine_nop(),
                 wait=[(semn, S.cnt[semn])])
            S.on('sync', lambda E: E.nop(),
                 wait=[(semn, S.cnt[semn])])

    with es:
        with nc.Block() as block:
            S.replay(block, sems)
    return nc


_CACHE = {}


def _get_nc():
    if "nc" not in _CACHE:
        nc = build_program()
        mybir.codegen_inst_isa_subclasses(nc)
        _CACHE["nc"] = nc
    return _CACHE["nc"]


def kernel(**inputs):
    x = np.asarray(inputs["L0_attn_x"])[0]
    wq = np.asarray(inputs["L0_attn_Wq"])
    wk = np.asarray(inputs["L0_attn_Wk"])
    wv = np.asarray(inputs["L0_attn_Wv"])
    wo = np.asarray(inputs["L0_attn_Wo"])
    gq = np.asarray(inputs["L0_attn_gq"]).reshape(128, 1)
    gk = np.asarray(inputs["L0_attn_gk"]).reshape(128, 1)
    wg = np.asarray(inputs["L0_ffn_wq_gate"])
    sgv = np.asarray(inputs["L0_ffn_s_gate"])
    wu = np.asarray(inputs["L0_ffn_wq_up"])
    suv = np.asarray(inputs["L0_ffn_s_up"])
    wd = np.asarray(inputs["L0_ffn_wq_down"])
    sdv = np.asarray(inputs["L0_ffn_s_down"])
    cv = np.asarray(inputs["L0_c"]).reshape(-1)
    wmod = np.asarray(inputs["L0_wmod"])

    woT = np.ascontiguousarray(wo.T.astype(np.float32))
    wgT = np.ascontiguousarray(wg.T.astype(np.int32))
    wuT = np.ascontiguousarray(wu.T.astype(np.int32))
    wdT = np.ascontiguousarray(wd.T.astype(np.int32))
    sg2 = np.ascontiguousarray(sgv.reshape(NHC, 128).T.astype(np.float32))
    su2 = np.ascontiguousarray(suv.reshape(NHC, 128).T.astype(np.float32))
    sd2 = np.ascontiguousarray(sdv.reshape(NCH, 128).T.astype(np.float32))

    in_maps = []
    for c in range(W):
        rows_h = slice(2 * c * 128, (2 * c + 2) * 128)
        in_maps.append({
            "xT": np.ascontiguousarray(x[c * TC:(c + 1) * TC].T),
            "wqT": np.ascontiguousarray(wq[rows_h].T.astype(np.float32)),
            "wkT": np.ascontiguousarray(wk[rows_h].T.astype(np.float32)),
            "wvT": np.ascontiguousarray(wv[rows_h].T.astype(np.float32)),
            "woT": woT,
            "wmodc": np.ascontiguousarray(wmod[256 * c:256 * (c + 1)]),
            "cslice": np.ascontiguousarray(
                cv[256 * c:256 * (c + 1)].reshape(2, 128).T),
            "gq": gq.astype(np.float32), "gk": gk.astype(np.float32),
            "wgT": wgT, "wuT": wuT, "wdT": wdT,
            "sg": sg2, "su": su2, "sdw": sd2,
        })

    nc = _get_nc()
    res = run_bass_kernel_spmd(nc, in_maps, core_ids=list(range(W)))
    out = np.empty((1, T, D), np.float32)
    for c in range(W):
        out[0, c * TC:(c + 1) * TC, :] = res.results[c]["outT"].T
    return out
